# version 02 (final)
# baseline (speedup 1.0000x reference)
import math
import numpy as np
import ml_dtypes
import concourse.bass as bass
import concourse.mybir as mybir
from concourse.bass_utils import run_bass_kernel_spmd
from concourse.aot_env import allow_ici_all_to_all

F32 = mybir.dt.float32
BF16 = mybir.dt.bfloat16
AX = mybir.AxisListType
ALU = mybir.AluOpType
AF = mybir.ActivationFunctionType

NC8 = 8
NF = 32
NTT = 8
SCALE = 1.0 / math.sqrt(128.0)
LNEPS = 1e-5
FP8MAX = 448.0
FEPS = 1e-12
NEGM = -1.0e9

QO = 0
KO = 2 * 512 * 512
VO = KO + 2 * 512 * 128
LKO = VO + 2 * 512 * 128
LVO = LKO + 2 * 512 * 512
BLK1 = LVO + 2 * 512 * 512
U2 = 512 * 128
BLK2 = 16 * U2

TRUNC = 99
RACEDET = False
WSLACK = 0
TESTSER = 0
DUMPXH = 0
LKKIND = "LK"
LKLHS = "ring"
LKLOAD = 1
DUMP = None
P3SEL = ("K", "V", "LV", "Q", "LQ", "LK")

AQ = 0
AK = 4096
ALK = 8192
AV = 12288
ALV = AV + 4224
APG = ALV + 4224
APL = APG + 1024
APTG = APL + 1024
APTL = APTG + 1024
AOS = APTL + 1024

QW = 1536

C_MV = 0
C_VE = 16
C_SD = 24
C_R = 32
C_NMR = 40
C_SCL = 48
C_NMS = 56
C_BN = 64
C_EPS = 112
C_GM = 113
C_GM2 = 114
C_GME = 115
C_XS = 116
C_XSB = 118
C_RDEV = 120
C_RDR = 130
C_GC = 138
C_MU = 140
C_BIAS = 144
C_LINV = 146
C_TMP = 150
C_SCC = 152
RMB = (64, 80)
C_D0 = 0
C_D1 = 32
C_DD = 64
C_W0 = 96
C_W1 = 128


def build():
    nc = bass.Bass("TRN2", target_bir_lowering=False, detect_race_conditions=RACEDET)

    xin = nc.dram_tensor("xin", [1024, 4096], F32, kind="ExternalInput")
    xout = nc.dram_tensor("xout", [1024, 4096], F32, kind="ExternalOutput")
    wq = nc.dram_tensor("wq", [4096, 4096], F32, kind="ExternalInput")
    wk = nc.dram_tensor("wk", [4096, 1024], F32, kind="ExternalInput")
    wv = nc.dram_tensor("wv", [4096, 1024], F32, kind="ExternalInput")
    wlq = nc.dram_tensor("wlq", [4096, 1024], F32, kind="ExternalInput")
    wlk = nc.dram_tensor("wlk", [1024, 4096], F32, kind="ExternalInput")
    wlv = nc.dram_tensor("wlv", [4096, 4096], F32, kind="ExternalInput")
    wo = nc.dram_tensor("wo", [4096, 4096], F32, kind="ExternalInput")
    wlo = nc.dram_tensor("wlo", [4096, 4096], F32, kind="ExternalInput")
    wffn = nc.dram_tensor("wffn", [4096, 4096], F32, kind="ExternalInput")
    wqkvT = nc.dram_tensor("wqkvT", [4096, 6144], F32, kind="ExternalInput")
    wprojT = nc.dram_tensor("wprojT", [4096, 4096], F32, kind="ExternalInput")
    cosin = nc.dram_tensor("cosin", [1024, 256], F32, kind="ExternalInput")
    masksin = nc.dram_tensor("masksin", [128, 896], BF16, kind="ExternalInput")
    onebin = nc.dram_tensor("onebin", [128, 4224], BF16, kind="ExternalInput")

    a1i = nc.dram_tensor("a1i", [8 * BLK1], BF16)
    a1o = nc.dram_tensor("a1o", [8 * BLK1], BF16)
    a2i = nc.dram_tensor("a2i", [8 * BLK2], BF16)
    a2o = nc.dram_tensor("a2o", [8 * BLK2], BF16)
    lqs = nc.dram_tensor("lqs", [8 * 128 * 1024], BF16)
    qkvb = nc.dram_tensor("qkvb", [1024 * 6144], BF16)
    bout = nc.dram_tensor("bout", [1024 * 4096], BF16)
    mxi = [nc.dram_tensor(f"mxi{i}", [1], F32) for i in range(2)]
    mxo = [nc.dram_tensor(f"mxo{i}", [1], F32) for i in range(2)]
    mxb = [nc.dram_tensor(f"mxb{i}", [2], F32) for i in range(2)]

    ctxs = []

    def alloc(cm):
        t = cm.__enter__()
        ctxs.append(cm)
        return t

    xres = alloc(nc.sbuf_tensor("xres", [128, NTT * 4096], F32))
    abig = alloc(nc.sbuf_tensor("abig", [128, 32768], BF16))
    wring = alloc(nc.sbuf_tensor("wring", [128, 2 * QW], BF16))
    xstage = alloc(nc.sbuf_tensor("xstage", [128, 2048], BF16))
    rtmp = alloc(nc.sbuf_tensor("rtmp", [128, 512], BF16))
    cosb = alloc(nc.sbuf_tensor("cosb", [128, 512], F32))
    maskr = alloc(nc.sbuf_tensor("maskr", [128, 896], BF16))
    st = alloc(nc.sbuf_tensor("stats", [128, 192], F32))
    pb = [alloc(nc.psum_tensor(f"pb{i}", [128, 512], F32)) for i in range(8)]

    XP = xres[:].ap[0][0]
    AP_ = abig[:].ap[0][0]
    WP = wring[:].ap[0][0]
    SP = xstage[:].ap[0][0]
    CP = cosb[:].ap[0][0]
    TP = st[:].ap[0][0]
    MP = maskr[:].ap[0][0]
    RP = rtmp[:].ap[0][0]
    PBP = pb[0][:].ap[0][0]

    def A(h, off, dims):
        return bass.AP(h, off, dims)

    def stc(c, w=1, p0=0, np_=128):
        return A(st, p0 * TP + c, [[TP, np_], [1, w]])

    def xrs(tt, c0, w):
        return A(xres, 4096 * tt + c0, [[XP, 128], [1, w]])

    def ab(off, w):
        return A(abig, off, [[AP_, 128], [1, w]])

    class Cnt:
        __slots__ = ("h", "v")

        def __init__(self, name):
            self.h = alloc(nc.semaphore(name))
            self.v = 0

        def inc(self, instr, n=1):
            instr.then_inc(self.h, n)
            self.v += n
            return self.v

        def wait(self, eng, v=None):
            eng.wait_ge(self.h, self.v if v is None else v)

    c_x = Cnt("c_x")
    c_al = Cnt("c_al")
    c_aw = Cnt("c_aw")
    c_a2w = Cnt("c_a2w")
    c_st = Cnt("c_st")
    c_w = Cnt("c_w")
    c_gd = Cnt("c_gd")
    c_cc = Cnt("c_cc")
    c_fw = Cnt("c_fw")
    c_te = Cnt("c_te")
    c_ev = Cnt("c_ev")
    c_rp = Cnt("c_rp")
    c_ve = Cnt("c_ve")
    c_se = Cnt("c_se")
    c_sx = Cnt("c_sx")
    c_ol = Cnt("c_ol")
    c_lq = Cnt("c_lq")
    c_pt = [Cnt(f"c_pt{i}") for i in range(4)]
    c_b = Cnt("c_b")

    block = nc.Block().__enter__()

    def sec(name):
        return getattr(block, name)

    ENGS = ("tensor", "vector", "scalar", "sync", "gpsimd")
    dma_owned = {"sync": [c_x, c_al, c_aw, c_a2w, c_st, c_lq],
                 "scalar": [c_sx, c_ol] + c_pt,
                 "gpsimd": [c_w, c_gd]}

    def barrier():
        for en in ENGS:
            @sec(en)
            def _(eng, en=en):
                for c in dma_owned.get(en, []):
                    if c.v:
                        eng.wait_ge(c.h, c.v)
                if en == "gpsimd" and c_cc.v:
                    eng.wait_ge(c_cc.h, c_cc.v)
                eng.drain()
                eng.nop().then_inc(c_b.h, 1)
        c_b.v += 5
        for en in ENGS:
            @sec(en)
            def _(eng):
                eng.wait_ge(c_b.h, c_b.v)

    qslot = [0]
    aw_slot = [0, 0]

    def quad_fetch(g, src_ap, dims_out):
        slot = qslot[0] % 2
        if qslot[0] >= 2:
            g.wait_ge(c_fw.h, qslot[0] - 1 + WSLACK)
        c_w.inc(g.dma_start(out=A(wring, QW * slot, dims_out), in_=src_ap), 16)
        qslot[0] += 1
        return slot, c_w.v

    def ph0():
        @sec("vector")
        def _(v):
            v.memset(stc(C_SCC, 1), SCALE)
            v.drain()
            c_ve.inc(v.nop())

        @sec("sync")
        def _(sy):
            for tt in range(NTT):
                c_x.inc(sy.dma_start(out=xrs(tt, 0, 4096),
                                     in_=A(xin, 4096 * 128 * tt, [[4096, 128], [1, 4096]])), 16)
            c_x.inc(sy.dma_start(out=maskr[:], in_=masksin.ap()), 16)

    def emit_ln(ffn_idx=None):
        @sec("vector")
        def _(v):
            for tt in range(NTT):
                v.drain()
                for k in range(8):
                    v.bn_stats(stc(C_BN + 6 * k, 6), xrs(tt, 512 * k, 512))
                v.drain()
                v.bn_aggr(stc(C_MV + 2 * tt, 2), stc(C_BN, 48))
                if ffn_idx is not None:
                    v.reduce_max(stc(C_TMP, 1), xrs(tt, 0, 4096), axis=AX.X)
                    v.tensor_reduce(stc(C_TMP + 1, 1), xrs(tt, 0, 4096), axis=AX.X, op=ALU.min)
                    v.drain()
                    v.tensor_tensor(stc(C_TMP, 1), stc(C_TMP, 1), stc(C_MV + 2 * tt, 1),
                                    ALU.subtract)
                    v.tensor_tensor(stc(C_TMP + 1, 1), stc(C_MV + 2 * tt, 1), stc(C_TMP + 1, 1),
                                    ALU.subtract)
                    v.tensor_tensor(stc(C_RDEV + tt, 1), stc(C_TMP, 1), stc(C_TMP + 1, 1), ALU.max)
            v.drain()
            for tt in range(NTT):
                v.tensor_scalar(stc(C_VE + tt, 1), stc(C_MV + 2 * tt + 1, 1), LNEPS, None,
                                ALU.add)
            v.drain()
            c_ve.inc(v.nop())
        sdv = c_ve.v

        @sec("scalar")
        def _(s):
            c_ve.wait(s, sdv)
            s.activation(stc(C_SD, 8), stc(C_VE, 8), AF.Sqrt)
            s.drain()
            c_se.inc(s.nop())
        sev = c_se.v

        @sec("vector")
        def _(v):
            c_se.wait(v, sev)
            v.reciprocal(stc(C_R, 8), stc(C_SD, 8))
            v.drain()
            for tt in range(NTT):
                v.tensor_tensor(stc(C_NMR + tt, 1), stc(C_MV + 2 * tt, 1), stc(C_R + tt, 1),
                                ALU.mult)
            v.drain()
            v.tensor_scalar(stc(C_NMR, 8), stc(C_NMR, 8), -1.0, None, ALU.mult)
            if ffn_idx is not None:
                v.tensor_tensor(stc(C_RDR, 8), stc(C_RDEV, 8), stc(C_R, 8), ALU.mult)
                v.drain()
                v.reduce_max(stc(C_GC, 1), stc(C_RDR, 8), axis=AX.X)
            v.drain()
            c_ve.inc(v.nop())
        

        if ffn_idx is not None:
            gcv = c_ve.v

            @sec("gpsimd")
            def _(g):
                c_ve.wait(g, gcv)
                g.tensor_reduce(stc(C_GM, 1, 0, 1), stc(C_GC, 1), axis=AX.C, op=ALU.max)
                g.drain()
                c_gd.inc(g.nop())
            grv = c_gd.v

            @sec("gpsimd")
            def _(g):
                g.wait_ge(c_gd.h, grv)
                c_gd.inc(g.dma_start(out=mxi[ffn_idx].ap(), in_=stc(C_GM, 1, 0, 1)), 16)
                c_gd.wait(g)
                c_cc.inc(g.collective_compute("AllReduce", ALU.max,
                                              replica_groups=[list(range(NC8))],
                                              ins=[mxi[ffn_idx].ap().opt()],
                                              outs=[mxo[ffn_idx].ap().opt()]))
                c_cc.wait(g)
                c_gd.inc(g.dma_start(out=stc(C_GM2, 1),
                                     in_=A(mxo[ffn_idx], 0, [[0, 128], [1, 1]])), 16)
            arv = c_gd.v

            @sec("vector")
            def _(v):
                v.wait_ge(c_gd.h, arv)
                v.memset(stc(C_EPS, 1), FEPS)
                v.drain()
                v.tensor_tensor(stc(C_GME, 1), stc(C_GM2, 1), stc(C_EPS, 1), ALU.max)
                v.drain()
                v.reciprocal(stc(C_XSB, 1), stc(C_GME, 1))
                v.drain()
                v.tensor_scalar(stc(C_XSB, 1), stc(C_XSB, 1), FP8MAX, None, ALU.mult)
                v.tensor_scalar(stc(C_XSB + 1, 1), stc(C_GME, 1), 1.0 / FP8MAX, None,
                                ALU.mult)
                v.drain()
                c_ve.inc(v.nop())
            bcv = c_ve.v

            @sec("vector")
            def _(v):
                c_ve.wait(v, bcv)
                v.tensor_tensor(stc(C_SCL, 8), stc(C_R, 8),
                                bass.broadcast_tensor_aps(stc(C_R, 8), stc(C_XSB, 1))[1], ALU.mult)
                v.tensor_tensor(stc(C_NMS, 8), stc(C_NMR, 8),
                                bass.broadcast_tensor_aps(stc(C_NMR, 8), stc(C_XSB, 1))[1],
                                ALU.mult)
                v.drain()
                c_ve.inc(v.nop())
            scl_c, nms_c = C_SCL, C_NMS
        else:
            scl_c, nms_c = C_R, C_NMR
        rdyv = c_ve.v

        @sec("scalar")
        def _(s):
            c_ve.wait(s, rdyv)
            k = 0
            for tt in range(NTT):
                for h in range(4):
                    po = 1024 * (k % 2)
                    ping = A(wring, po, [[WP, 128], [1, 1024]])
                    if k >= 2:
                        s.wait_ge(c_sx.h, c_sx.v - 128)
                    s.activation(ping, xrs(tt, 1024 * h, 1024), AF.Identity,
                                 bias=stc(nms_c + tt, 1), scale=stc(scl_c + tt, 1))
                    s.drain()
                    for kk in range(8):
                        f = 8 * h + kk
                        src = A(wring, po + 128 * kk, [[WP, 128], [1, 128]])
                        c_sx.inc(s.dma_start(out=ab(1024 * f + 128 * tt, 128), in_=src,
                                             transpose=True), 16)
                    k += 1

    def rope_write(kind, jc, tt, ss, cosv, par, evv):
        b, ps = tt // 4, tt % 4
        stg4 = A(xstage, 512 * ss, [[SP, 128], [128, 4], [64, 2], [1, 64]])
        if kind in ("Q", "K", "LK"):
            @sec("vector")
            def _(v):
                v.wait_ge(c_x.h, cosv)
                cso = 256 * par
                swap = A(xstage, 512 * ss + 64, [[SP, 128], [128, 4], [-64, 2], [1, 64]])
                sinb = A(cosb, cso + 128, [[CP, 128], [0, 4], [64, 2], [1, 64]])
                cosbc = A(cosb, cso, [[CP, 128], [0, 4], [64, 2], [1, 64]])
                tmp4 = A(rtmp, 0, [[RP, 128], [128, 4], [64, 2], [1, 64]])
                v.tensor_tensor(tmp4, swap, sinb, ALU.mult)
                v.tensor_tensor(stg4, stg4, cosbc, ALU.mult)
                v.tensor_tensor(stg4, stg4, tmp4, ALU.add)
                v.drain()
                c_rp.inc(v.nop())
            rpv = ("rp", c_rp.v)
        else:
            rpv = ("ev", evv)

        @sec("sync")
        def _(sy):
            if rpv[0] == "rp":
                sy.wait_ge(c_rp.h, rpv[1])
            else:
                sy.wait_ge(c_ev.h, rpv[1])
            sb = A(xstage, 512 * ss, [[SP, 128], [1, 512]])
            if kind in ("Q", "LK", "LV"):
                base = {"Q": QO, "LK": LKO, "LV": LVO}[kind]
                off = jc * BLK1 + base + b * (512 * 512) + ps * 128 * 512
                c_aw.inc(sy.dma_start(out=A(a1i, off, [[512, 128], [1, 512]]), in_=sb), 16)
            elif kind in ("K", "V"):
                base = KO if kind == "K" else VO
                for gi in range(4):
                    gd = 4 * jc + gi
                    off = gd * BLK1 + base + b * (512 * 128) + ps * 128 * 128
                    c_aw.inc(sy.dma_start(out=A(a1i, off, [[128, 128], [1, 128]]),
                                          in_=A(xstage, 512 * ss + 128 * gi,
                                                [[SP, 128], [1, 128]])), 16)
            elif kind == "QKV":
                off = tt * 128 * 6144 + 512 * jc
                c_aw.inc(sy.dma_start(out=A(qkvb, off, [[6144, 128], [1, 512]]), in_=sb), 16)
            elif kind == "LQ":
                lc, th = jc // 2, jc % 2
                off = lc * 131072 + 512 * th
                c_aw.inc(sy.dma_start(out=A(lqs, off, [[1024, 128], [1, 512]]), in_=sb), 16)

    ssi = [0]
    rope_seq = [0]
    cos_guard = [0, 0]

    def evac_tile(kind, jc, tt, te_target, bank, roped):
        ss = ssi[0] % 2
        cosv = None
        par = 0
        if roped:
            par = rope_seq[0] % 2
            rope_seq[0] += 1
            gv = cos_guard[par]

            @sec("sync")
            def _(sy):
                if gv:
                    sy.wait_ge(c_rp.h, gv)
                c_x.inc(sy.dma_start(out=A(cosb, 256 * par, [[CP, 128], [1, 256]]),
                                     in_=A(cosin, 256 * 128 * tt, [[256, 128], [1, 256]])), 16)
            cosv = c_x.v

        @sec("vector")
        def _(v):
            v.wait_ge(c_te.h, te_target)
            if aw_slot[ss]:
                v.wait_ge(c_aw.h, aw_slot[ss])
            v.tensor_copy(A(xstage, 512 * ss, [[SP, 128], [1, 512]]), pb[bank][:, 0:512])
            v.drain()
            c_ev.inc(v.nop())
        rope_write(kind, jc, tt, ss, cosv, par, c_ev.v)
        if roped:
            cos_guard[par] = c_rp.v
        aw_slot[ss] = c_aw.v
        ssi[0] += 1

    def emit_proj_std(w_h, in_rows, out_cols, kind):
        nfw = in_rows // 128
        njc = out_cols // 512
        roped = kind in ("Q", "K", "LK")
        evbase = c_ev.v
        for jc in range(njc):
            tebase = c_te.v
            for fq in range((nfw + 2) // 3):
                nfl = min(3, nfw - 3 * fq)
                res = [None]

                @sec("gpsimd")
                def _(g, fq=fq, jc=jc, nfl=nfl):
                    off = (384 * fq) * out_cols + 512 * jc
                    src = A(w_h, off, [[out_cols, 128], [128 * out_cols, nfl], [1, 512]])
                    res[0] = quad_fetch(g, src, [[WP, 128], [512, nfl], [1, 512]])
                slot, wv = res[0]

                @sec("tensor")
                def _(t, fq=fq, jc=jc, nfl=nfl, slot=slot, wv=wv, evbase=evbase):
                    t.wait_ge(c_w.h, wv)
                    last_f = 3 * fq + nfl - 1
                    for df in range(nfl):
                        f = 3 * fq + df
                        for tt in range(NTT):
                            if f == 0:
                                t.wait_ge(c_ev.h,
                                          evbase + 8 * (jc - 1) + tt + 1 if jc > 0 else evbase)
                            ins = t.matmul(pb[tt][:, 0:512],
                                           ab(1024 * f + 128 * tt, 128),
                                           A(wring, QW * slot + 512 * df, [[WP, 128], [1, 512]]),
                                           start=(f == 0), stop=(f == nfw - 1))
                            if f == nfw - 1:
                                c_te.inc(ins)
                            if f == last_f and tt == NTT - 1:
                                if f == nfw - 1:
                                    t.wait_ge(c_te.h, c_te.v)
                                    c_fw.inc(t.nop())
                                else:
                                    c_fw.inc(ins)
            for tt in range(NTT):
                evac_tile(kind, jc, tt, tebase + tt + 1, tt, roped)
            if TESTSER:
                barrier()

    def emit_lq():
        for th in (0, 1):
            evbase0 = c_ev.v
            tebase = c_te.v
            for f in range(32):
                res = [None]

                @sec("gpsimd")
                def _(g, f=f):
                    src = A(wlq, (128 * f) * 1024, [[1024, 128], [1, 1024]])
                    res[0] = quad_fetch(g, src, [[WP, 128], [1, 1024]])
                slot, wv = res[0]

                @sec("tensor")
                def _(t, f=f, th=th, slot=slot, wv=wv, evbase0=evbase0):
                    t.wait_ge(c_w.h, wv)
                    for lc in range(8):
                        if f == 0:
                            t.wait_ge(c_ev.h, evbase0)
                        ins = t.matmul(pb[lc][:, 0:512],
                                       A(wring, QW * slot + 128 * lc, [[WP, 128], [1, 128]]),
                                       ab(1024 * f + 512 * th, 512),
                                       start=(f == 0), stop=(f == 31))
                        if f == 31:
                            c_te.inc(ins)
                        if lc == 7:
                            if f == 31:
                                t.wait_ge(c_te.h, c_te.v)
                                c_fw.inc(t.nop())
                            else:
                                c_fw.inc(ins)
            for lc in range(8):
                evac_tile("LQ", 2 * lc + th, 0, tebase + lc + 1, lc, False)

    def emit_lk():
        evbase = c_ev.v
        wavete = [0]
        lk_state = {}
        for jc in range(8):
            stop_targets = [0] * NTT
            for l8 in range(8):
                if LKLOAD:
                    @sec("sync")
                    def _(sy, l8=l8, jc=jc):
                        if jc + l8 > 0:
                            sy.wait_ge(c_te.h, wavete[0])
                        for tt in range(NTT):
                            off = l8 * 131072 + 128 * tt
                            c_lq.inc(sy.dma_start(out=A(xstage, 1024 + 128 * tt,
                                                        [[SP, 128], [1, 128]]),
                                                  in_=A(lqs, off, [[1024, 128], [1, 128]])), 16)
                lqv = c_lq.v
                if l8 in (0, 3, 6):
                    nfl = min(3, 8 - l8)
                    res = [None]

                    @sec("gpsimd")
                    def _(g, l8=l8, jc=jc, nfl=nfl):
                        off = (128 * l8) * 4096 + 512 * jc
                        src = A(wlk, off, [[4096, 128], [128 * 4096, nfl], [1, 512]])
                        res[0] = quad_fetch(g, src, [[WP, 128], [512, nfl], [1, 512]])
                    lk_state["slot"], lk_state["wv"] = res[0]
                    lk_state["base"] = l8
                slot, wv, fb = lk_state["slot"], lk_state["wv"], lk_state["base"]

                @sec("tensor")
                def _(t, l8=l8, jc=jc, slot=slot, wv=wv, fb=fb, lqv=lqv, evbase=evbase):
                    t.wait_ge(c_w.h, wv)
                    t.wait_ge(c_lq.h, lqv)
                    for tt in range(NTT):
                        if l8 == 0:
                            t.wait_ge(c_ev.h,
                                      evbase + 8 * (jc - 1) + tt + 1 if jc > 0 else evbase)
                        lhs_ = (A(xstage, 1024 + 128 * tt, [[SP, 128], [1, 128]])
                                if LKLHS == "ring" else ab(1024 * l8 + 128 * tt, 128))
                        ins = t.matmul(pb[tt][:, 0:512], lhs_,
                                       A(wring, QW * slot + 512 * (l8 - fb), [[WP, 128], [1, 512]]),
                                       start=(l8 == 0), stop=(l8 == 7))
                        if l8 == 7:
                            stop_targets[tt] = c_te.inc(ins)
                            if tt == NTT - 1:
                                wavete[0] = c_te.v
                        elif tt == NTT - 1:
                            wavete[0] = c_te.inc(ins)
                    if l8 in (2, 5, 7):
                        t.wait_ge(c_te.h, wavete[0])
                        c_fw.inc(t.nop())
            for tt in range(NTT):
                evac_tile(LKKIND, jc, tt, stop_targets[tt], tt, LKKIND == "LK")

    def emit_ffn():
        evbase = c_ev.v
        for jc in range(8):
            tebase = c_te.v
            for fq in range(11):
                nfl = min(3, 32 - 3 * fq)
                res = [None]

                @sec("gpsimd")
                def _(g, fq=fq, jc=jc, nfl=nfl):
                    off = (384 * fq) * 4096 + 512 * jc
                    src = A(wffn, off, [[4096, 128], [128 * 4096, nfl], [1, 512]])
                    res[0] = quad_fetch(g, src, [[WP, 128], [512, nfl], [1, 512]])
                slot, wv = res[0]

                @sec("tensor")
                def _(t, fq=fq, jc=jc, nfl=nfl, slot=slot, wv=wv, evbase=evbase):
                    t.wait_ge(c_w.h, wv)
                    last_f = 3 * fq + nfl - 1
                    for df in range(nfl):
                        f = 3 * fq + df
                        for tt in range(NTT):
                            if f == 0:
                                t.wait_ge(c_ev.h,
                                          evbase + 8 * (jc - 1) + tt + 1 if jc > 0 else evbase)
                            ins = t.matmul(pb[tt][:, 0:512],
                                           ab(1024 * f + 128 * tt, 128),
                                           A(wring, QW * slot + 512 * df, [[WP, 128], [1, 512]]),
                                           start=(f == 0), stop=(f == 31))
                            if f == 31:
                                c_te.inc(ins)
                            if f == last_f and tt == NTT - 1:
                                if f == 31:
                                    t.wait_ge(c_te.h, c_te.v)
                                    c_fw.inc(t.nop())
                                else:
                                    c_fw.inc(ins)
            for tt in range(NTT):
                @sec("vector")
                def _(v, tt=tt, jc=jc, tebase=tebase):
                    v.wait_ge(c_te.h, tebase + tt + 1)
                    xsl = xrs(tt, 512 * jc, 512)
                    v.scalar_tensor_tensor(xsl, pb[tt][:, 0:512], stc(C_XSB + 1, 1), xsl,
                                           ALU.mult, ALU.add)
                    v.drain()
                    c_ev.inc(v.nop())

    def emit_a2a(iap, oap, awcnt):
        @sec("gpsimd")
        def _(g):
            g.wait_ge(awcnt.h, awcnt.v)
            with allow_ici_all_to_all():
                ins = g.collective_compute("AllToAll", ALU.bypass,
                                           replica_groups=[list(range(NC8))],
                                           ins=[iap.ap().opt()], outs=[oap.ap().opt()])
            c_cc.inc(ins)

    ptslot = {(ty, sl): (0, 0) for ty in (0, 1) for sl in (0, 1)}
    osl = {(ty, sl): 0 for ty in (0, 1) for sl in (0, 1)}
    expbank = {(ty, sl): 0 for ty in (0, 1) for sl in (0, 1)}
    ptn = [0]

    def mask_ap(rel):
        return A(maskr, 384 - 128 * rel, [[MP, 128], [1, 512]])

    def emit_attention():
        @sec("sync")
        def _(sy):
            for base in (AV, ALV):
                c_al.inc(sy.dma_start(out=ab(base, 4224),
                                      in_=A(onebin, 0, [[4224, 128], [1, 4224]])), 16)
        for b in (0, 1):
            @sec("sync")
            def _(sy, b=b):
                if c_te.v:
                    sy.wait_ge(c_te.h, c_te.v)
                for r in range(8):
                    off = r * BLK1 + KO + b * 65536
                    c_al.inc(sy.dma_start(out=ab(AK + 512 * r, 512),
                                          in_=A(a1o, off, [[128, 512], [1, 128]]),
                                          transpose=True), 16)
                for r in range(8):
                    off = r * BLK1 + VO + b * 65536
                    c_al.inc(sy.dma_start(out=A(abig, AV + 528 * r,
                                                [[AP_, 128], [132, 4], [1, 128]]),
                                          in_=A(a1o, off, [[128, 128], [16384, 4], [1, 128]])), 16)
            for hh in range(4):
                @sec("sync")
                def _(sy, b=b, hh=hh):
                    if not (b == 0 and hh == 0):
                        sy.wait_ge(c_te.h, c_te.v)
                    for r in range(8):
                        off = r * BLK1 + QO + b * 262144 + hh * 128
                        c_al.inc(sy.dma_start(out=ab(AQ + 512 * r, 512),
                                              in_=A(a1o, off, [[512, 512], [1, 128]]),
                                              transpose=True), 16)
                    for r in range(8):
                        off = r * BLK1 + LKO + b * 262144 + hh * 128
                        c_al.inc(sy.dma_start(out=ab(ALK + 512 * r, 512),
                                              in_=A(a1o, off, [[512, 512], [1, 128]]),
                                              transpose=True), 16)
                    for r in range(8):
                        off = r * BLK1 + LVO + b * 262144 + hh * 128
                        c_al.inc(sy.dma_start(out=A(abig, ALV + 528 * r,
                                                    [[AP_, 128], [132, 4], [1, 128]]),
                                              in_=A(a1o, off, [[512, 128], [65536, 4], [1, 128]])),
                                 16)
                emit_unit_pair(b, hh, c_al.v)

    def emit_unit_pair(b, hh, alv):
        U = [dict(kc=AK, vc=AV, acc=2, sb=(0, 1), ps=APG, pt=APTG, mcol=C_MU,
                  bcol=C_BIAS, lcol=C_LINV, typ=0, rmbase=RMB[0]),
             dict(kc=ALK, vc=ALV, acc=5, sb=(3, 4), ps=APL, pt=APTL, mcol=C_MU + 1,
                  bcol=C_BIAS + 1, lcol=C_LINV + 1, typ=1, rmbase=RMB[1])]
        uev = {0: c_ev.v, 1: c_ev.v}
        for i in range(32):
            nst = i // 4
            rel = i % 4
            nstrips = nst + 1
            p1te = {}
            p1ve = {}
            for s in range(nstrips):
                for u in U:
                    @sec("tensor")
                    def _(t, u=u, s=s, i=i):
                        t.wait_ge(c_al.h, alv)
                        if s >= 2:
                            t.wait_ge(c_ve.h, p1ve[(u["typ"], s - 2)])
                        else:
                            ev_ = expbank[(u["typ"], s % 2)]
                            if ev_:
                                t.wait_ge(c_se.h, ev_)
                        ins = t.matmul(pb[u["sb"][s % 2]][:, 0:512],
                                       ab(AQ + 128 * i, 128),
                                       ab(u["kc"] + 512 * s, 512),
                                       start=True, stop=True)
                        c_te.inc(ins)
                    p1te[(u["typ"], s)] = c_te.v

                    @sec("vector")
                    def _(v, u=u, s=s, nst=nst, rel=rel):
                        v.wait_ge(c_te.h, p1te[(u["typ"], s)])
                        bank = pb[u["sb"][s % 2]]
                        if s == nst:
                            v.tensor_tensor(bank[:, 0:512], bank[:, 0:512], mask_ap(rel), ALU.add)
                        v.reduce_max(stc(u["rmbase"] + s, 1), bank[:, 0:512], axis=AX.X)
                        v.drain()
                        c_ve.inc(v.nop())
                    p1ve[(u["typ"], s)] = c_ve.v
            for u in U:
                @sec("vector")
                def _(v, u=u, nstrips=nstrips):
                    v.drain()
                    v.reduce_max(stc(u["mcol"], 1), stc(u["rmbase"], nstrips), axis=AX.X)
                    v.drain()
                    v.tensor_scalar(stc(u["bcol"], 1), stc(u["mcol"], 1), -1.0, None, ALU.mult)
                    v.drain()
                    c_ve.inc(v.nop())
                u["biasv"] = c_ve.v
            p2te = {}
            p2se = {}
            pvte = {}
            p2mask = {}
            for s in range(nstrips):
                for u in U:
                    @sec("tensor")
                    def _(t, u=u, s=s, i=i):
                        t.wait_ge(c_ve.h, u["biasv"])
                        if s >= 2:
                            t.wait_ge(c_se.h, p2se[(u["typ"], s - 2)])
                        ins = t.matmul(pb[u["sb"][s % 2]][:, 0:512],
                                       ab(AQ + 128 * i, 128),
                                       ab(u["kc"] + 512 * s, 512),
                                       start=True, stop=True)
                        c_te.inc(ins)
                    p2te[(u["typ"], s)] = c_te.v
                    @sec("vector")
                    def _(v, u=u, s=s, nst=nst, rel=rel):
                        v.wait_ge(c_te.h, p2te[(u["typ"], s)])
                        bank = pb[u["sb"][s % 2]]
                        if s == nst:
                            v.tensor_tensor(bank[:, 0:512], bank[:, 0:512], mask_ap(rel), ALU.add)
                        v.tensor_scalar(bank[:, 0:512], bank[:, 0:512], stc(u["bcol"], 1), SCALE,
                                        ALU.add, ALU.mult)
                        v.drain()
                        c_ve.inc(v.nop())
                    p2mask[u["typ"]] = c_ve.v
                    rot = ptn[0] % 4
                    ptn[0] += 1

                    @sec("scalar")
                    def _(se, u=u, s=s, nst=nst, rot=rot):
                        se.wait_ge(c_ve.h, p2mask[u["typ"]])
                        prot, pval = ptslot[(u["typ"], s % 2)]
                        if pval:
                            se.wait_ge(c_pt[prot].h, pval)
                        se.activation(ab(u["ps"] + 512 * (s % 2), 512),
                                      pb[u["sb"][s % 2]][:, 0:512], AF.Exp)
                        se.drain()
                        c_se.inc(se.nop())
                        if s >= 2:
                            se.wait_ge(c_te.h, pvte[(u["typ"], s - 2)])
                        for j4 in range(4):
                            d = se.dma_start(out=ab(u["pt"] + 512 * (s % 2) + 128 * j4, 128),
                                             in_=ab(u["ps"] + 512 * (s % 2) + 128 * j4, 128),
                                             transpose=True)
                            c_pt[rot].inc(d, 16)
                    p2se[(u["typ"], s)] = c_se.v
                    expbank[(u["typ"], s % 2)] = c_se.v
                    ptslot[(u["typ"], s % 2)] = (rot, c_pt[rot].v)
                    ptv = c_pt[rot].v

                    @sec("tensor")
                    def _(t, u=u, s=s, nst=nst, rot=rot, ptv=ptv):
                        t.wait_ge(c_pt[rot].h, ptv)
                        if s == 0:
                            t.wait_ge(c_ev.h, uev[u["typ"]])
                        for j4 in range(4):
                            ins = t.matmul(pb[u["acc"]][:, 0:129],
                                           ab(u["pt"] + 512 * (s % 2) + 128 * j4, 128),
                                           A(abig, u["vc"] + 132 * (4 * s + j4),
                                             [[AP_, 128], [1, 129]]),
                                           start=(s == 0 and j4 == 0), stop=(s == nst and j4 == 3))
                            if j4 == 3:
                                c_te.inc(ins)
                    pvte[(u["typ"], s)] = c_te.v
            for u in U:
                @sec("vector")
                def _(v, u=u, i=i):
                    v.wait_ge(c_te.h, pvte[(u["typ"], nst)])
                    ov = osl[(u["typ"], i % 2)]
                    if ov:
                        v.wait_ge(c_a2w.h, ov)
                    oslot = AOS + 128 * (2 * u["typ"] + (i % 2))
                    v.reciprocal(stc(u["lcol"], 1), A(pb[u["acc"]], 128, [[PBP, 128], [1, 1]]))
                    v.drain()
                    v.tensor_scalar(ab(oslot, 128), pb[u["acc"]][:, 0:128],
                                    stc(u["lcol"], 1), None, ALU.mult)
                    v.drain()
                    c_ev.inc(v.nop())
                uev[u["typ"]] = c_ev.v
                evv = c_ev.v

                @sec("sync")
                def _(sy, u=u, i=i, evv=evv, b=b, hh=hh):
                    sy.wait_ge(c_ev.h, evv)
                    d = i // 4
                    off = d * BLK2 + u["typ"] * 524288 + hh * 131072 + b * 65536 + (i % 4) * 16384
                    oslot = AOS + 128 * (2 * u["typ"] + (i % 2))
                    c_a2w.inc(sy.dma_start(out=A(a2i, off, [[128, 128], [1, 128]]),
                                           in_=ab(oslot, 128)), 16)
                osl[(u["typ"], i % 2)] = c_a2w.v

    opev = [0] * 4

    def emit_oproj(src_kind):
        n_e = 64 if src_kind == "o" else 32
        for half in (0, 1):
            @sec("scalar")
            def _(se, half=half):
                if c_te.v:
                    se.wait_ge(c_te.h, c_te.v)
                for e in range(n_e):
                    for q4 in range(4):
                        tt = 4 * half + q4
                        if src_kind == "o":
                            typ, h = (0, e) if e < 32 else (1, e - 32)
                            r, hh2 = h // 4, h % 4
                            bb, ps = tt // 4, tt % 4
                            off = r * BLK2 + typ * 524288 + hh2 * 131072 + bb * 65536 + ps * 16384
                            src = A(a2o, off, [[128, 128], [1, 128]])
                        else:
                            off = tt * 128 * 4096 + 128 * e
                            src = A(bout, off, [[4096, 128], [1, 128]])
                        c_ol.inc(se.dma_start(out=ab(512 * e + 128 * q4, 128), in_=src,
                                              transpose=True), 16)
            olv = c_ol.v
            for j in range(8):
                tebase = c_te.v
                for ep in range(n_e // 2):
                    res = [None]

                    @sec("gpsimd")
                    def _(g, ep=ep, j=j):
                        e0 = 2 * ep
                        if src_kind == "o":
                            w_h = wo if e0 < 32 else wlo
                            eb = e0 % 32
                        else:
                            w_h = wprojT
                            eb = e0
                        src = A(w_h, eb * 128 * 4096 + 512 * j,
                                [[4096, 128], [128 * 4096, 2], [1, 512]])
                        res[0] = quad_fetch(g, src, [[WP, 128], [512, 2], [1, 512]])
                    slot, wv = res[0]

                    @sec("tensor")
                    def _(t, ep=ep, j=j, slot=slot, wv=wv, olv=olv, half=half):
                        t.wait_ge(c_w.h, wv)
                        if ep == 0 and j == 0:
                            t.wait_ge(c_ol.h, olv)
                        for de in (0, 1):
                            e = 2 * ep + de
                            for tt4 in range(4):
                                if e == 0 and (j > 0 or half == 1):
                                    t.wait_ge(c_ev.h, opev[tt4])
                                ins = t.matmul(pb[tt4][:, 0:512],
                                               ab(512 * e + 128 * tt4, 128),
                                               A(wring, QW * slot + 512 * de,
                                                 [[WP, 128], [1, 512]]),
                                               start=(e == 0), stop=(e == n_e - 1))
                                if e == n_e - 1:
                                    c_te.inc(ins)
                                if de == 1 and tt4 == 3:
                                    if e == n_e - 1:
                                        t.wait_ge(c_te.h, c_te.v)
                                        c_fw.inc(t.nop())
                                    else:
                                        c_fw.inc(ins)
                for tt4 in range(4):
                    @sec("vector")
                    def _(v, tt4=tt4, j=j, half=half, tebase=tebase):
                        v.wait_ge(c_te.h, tebase + tt4 + 1)
                        xsl = xrs(4 * half + tt4, 512 * j, 512)
                        v.tensor_tensor(xsl, xsl, pb[tt4][:, 0:512], ALU.add)
                        v.drain()
                        c_ev.inc(v.nop())
                    opev[tt4] = c_ev.v

    def emit_attnb_mid():
        for pt_ in range(4):
            @sec("sync")
            def _(sy, pt_=pt_):
                if pt_ > 0:
                    sy.wait_ge(c_aw.h, c_aw.v)
                c_al.inc(sy.dma_start(out=ab(0, 6144),
                                      in_=A(qkvb, pt_ * 128 * 6144, [[6144, 128], [1, 6144]])), 16)
                c_al.inc(sy.dma_start(out=ab(6144, 6144),
                                      in_=A(qkvb, (512 + 128 * pt_) * 6144,
                                            [[6144, 128], [1, 6144]])), 16)
            alv = c_al.v

            @sec("vector")
            def _(v, alv=alv):
                v.wait_ge(c_al.h, alv)
                for g8 in range(8):
                    q1 = A(abig, 6144 + 768 * g8, [[AP_, 128], [128, 4], [1, 128]])
                    k0 = A(abig, 768 * g8 + 512, [[AP_, 128], [0, 4], [1, 128]])
                    k1 = A(abig, 6144 + 768 * g8 + 512, [[AP_, 128], [0, 4], [1, 128]])
                    p3 = A(pb[g8], 0, [[PBP, 128], [128, 4], [1, 128]])
                    v.tensor_tensor(p3, q1, k0, ALU.mult)
                    v.reduce_sum(stc(C_D0 + 4 * g8, 4), p3, axis=AX.X)
                    v.tensor_tensor(p3, q1, k1, ALU.mult)
                    v.reduce_sum(stc(C_D1 + 4 * g8, 4), p3, axis=AX.X)
                v.drain()
                v.tensor_tensor(stc(C_DD, 32), stc(C_D0, 32), stc(C_D1, 32), ALU.subtract)
                v.drain()
                c_ve.inc(v.nop())
            vev = c_ve.v

            @sec("scalar")
            def _(se, vev=vev):
                se.wait_ge(c_ve.h, vev)
                se.activation(stc(C_W0, 32), stc(C_DD, 32), AF.Sigmoid, scale=SCALE)
                se.activation(stc(C_W1, 32), stc(C_DD, 32), AF.Sigmoid, scale=-SCALE)
                se.drain()
                c_se.inc(se.nop())
            sev = c_se.v

            @sec("vector")
            def _(v, sev=sev):
                v.wait_ge(c_se.h, sev)
                o0 = A(abig, 12288, [[AP_, 128], [512, 8], [128, 4], [1, 128]])
                o1a = A(abig, 16384, [[AP_, 128], [512, 8], [128, 4], [1, 128]])
                o1b = A(abig, 20480, [[AP_, 128], [512, 8], [128, 4], [1, 128]])
                v0 = A(abig, 640, [[AP_, 128], [768, 8], [0, 4], [1, 128]])
                v1 = A(abig, 6144 + 640, [[AP_, 128], [768, 8], [0, 4], [1, 128]])
                w0 = A(st, C_W0, [[TP, 128], [4, 8], [1, 4], [0, 128]])
                w1 = A(st, C_W1, [[TP, 128], [4, 8], [1, 4], [0, 128]])
                v.tensor_copy(o0, v0)
                v.tensor_tensor(o1a, v0, w0, ALU.mult)
                v.tensor_tensor(o1b, v1, w1, ALU.mult)
                v.tensor_tensor(o1a, o1a, o1b, ALU.add)
                v.drain()
                c_ve.inc(v.nop())
            vev2 = c_ve.v

            @sec("sync")
            def _(sy, pt_=pt_, vev2=vev2):
                sy.wait_ge(c_ve.h, vev2)
                c_aw.inc(sy.dma_start(out=A(bout, (128 * pt_) * 4096, [[4096, 128], [1, 4096]]),
                                      in_=ab(12288, 4096)), 16)
                c_aw.inc(sy.dma_start(out=A(bout, (512 + 128 * pt_) * 4096,
                                            [[4096, 128], [1, 4096]]),
                                      in_=ab(16384, 4096)), 16)

    def emit_dump():
        src, off, ndt = (DUMP if len(DUMP) == 3 else (DUMP[0], DUMP[1], 8))
        if src == "stats":
            @sec("vector")
            def _(v):
                v.tensor_copy(xrs(0, 0, 192), stc(0, 192))
                v.drain()
                c_ve.inc(v.nop())

            @sec("sync")
            def _(sy):
                sy.wait_ge(c_ve.h, c_ve.v)
            return
        if src == "abig":
            @sec("vector")
            def _(v):
                for tt in range(ndt):
                    v.tensor_copy(xrs(tt, 0, 4096), ab(4096 * tt + off, 4096))
                v.drain()
                c_ve.inc(v.nop())

            @sec("sync")
            def _(sy):
                sy.wait_ge(c_ve.h, c_ve.v)
            return
        srcmap = {"a1o": a1o, "a1i": a1i, "a2i": a2i, "a2o": a2o, "qkvb": qkvb,
                  "bout": bout, "lqs": lqs}
        h = srcmap[src]

        @sec("gpsimd")
        def _(g):
            for tt in range(ndt):
                c_gd.inc(g.dma_start(out=xrs(tt, 0, 4096),
                                     in_=A(h, off + tt * 128 * 4096,
                                           [[4096, 128], [1, 4096]])), 16)
            g.wait_ge(c_gd.h, c_gd.v)

    def ph_store():
        @sec("sync")
        def _(sy):
            for tt in range(NTT):
                c_st.inc(sy.dma_start(out=A(xout, 4096 * 128 * tt, [[4096, 128], [1, 4096]]),
                                      in_=xrs(tt, 0, 4096)), 16)
            sy.wait_ge(c_st.h, c_st.v)

    ph = [0]

    def gate():
        ph[0] += 1
        return ph[0] <= TRUNC

    if gate():
        ph0()
        barrier()
    if gate():
        emit_ln(None)
        barrier()
        if DUMPXH:
            @sec("sync")
            def _(sy):
                c_x.inc(sy.dma_start(out=A(qkvb, 0, [[32768, 128], [1, 32768]]),
                                     in_=ab(0, 32768)), 16)
            barrier()
    if gate():
        if "K" in P3SEL:
            emit_proj_std(wk, 4096, 1024, "K")
        if "V" in P3SEL:
            emit_proj_std(wv, 4096, 1024, "V")
        if "LV" in P3SEL:
            emit_proj_std(wlv, 4096, 4096, "LV")
        if "Q" in P3SEL:
            emit_proj_std(wq, 4096, 4096, "Q")
        if "LQ" in P3SEL:
            emit_lq()
        barrier()
        if "LK" in P3SEL:
            emit_lk()
        barrier()
    if gate():
        emit_a2a(a1i, a1o, c_aw)
        barrier()
    if gate():
        emit_attention()
        barrier()
    if gate():
        emit_a2a(a2i, a2o, c_a2w)
        barrier()
    if gate():
        emit_oproj("o")
        barrier()
    if gate():
        emit_ln(0)
        barrier()
        emit_ffn()
        barrier()
    if gate():
        emit_ln(None)
        barrier()
        emit_proj_std(wqkvT, 4096, 6144, "QKV")
        barrier()
        emit_attnb_mid()
        barrier()
        emit_oproj("b")
        barrier()
    if gate():
        emit_ln(1)
        barrier()
        emit_ffn()
        barrier()
    if DUMP is not None:
        emit_dump()
        barrier()
    ph_store()

    block.__exit__(None, None, None)
    for cm in reversed(ctxs):
        cm.__exit__(None, None, None)
    return nc


def _rope_tables(core):
    inv = (1.0 / (10000.0 ** (np.arange(0, 128, 2, dtype=np.float64) / 128.0)))
    pos = np.arange(512 * core, 512 * core + 512, dtype=np.float64)
    fr = np.outer(pos, inv)
    emb = np.concatenate([fr, fr], axis=1)
    cos = np.cos(emb).astype(np.float32)
    sin = np.sin(emb).astype(np.float32)
    sinp = sin.copy()
    sinp[:, :64] = -sinp[:, :64]
    ct = np.concatenate([cos, sinp], axis=1).astype(np.float32)
    return np.ascontiguousarray(np.concatenate([ct, ct], axis=0))


def _masks():
    q = np.arange(128)[:, None]
    jj = np.arange(896)[None, :]
    return np.where(jj <= q + 384, 0.0, NEGM).astype(ml_dtypes.bfloat16)


def kernel(**inputs):
    x = np.asarray(inputs["L0_attn_a_hidden_states"], dtype=np.float32)
    wq = np.ascontiguousarray(np.asarray(inputs["L0_attn_a_w_q"], dtype=np.float32))
    wk = np.ascontiguousarray(np.asarray(inputs["L0_attn_a_w_k"], dtype=np.float32))
    wv = np.ascontiguousarray(np.asarray(inputs["L0_attn_a_w_v"], dtype=np.float32))
    wo = np.ascontiguousarray(np.asarray(inputs["L0_attn_a_w_o"], dtype=np.float32))
    wlq = np.ascontiguousarray(np.asarray(inputs["L0_attn_a_w_lq"], dtype=np.float32))
    wlk = np.ascontiguousarray(np.asarray(inputs["L0_attn_a_w_lk"], dtype=np.float32))
    wlv = np.ascontiguousarray(np.asarray(inputs["L0_attn_a_w_lv"], dtype=np.float32))
    wlo = np.ascontiguousarray(np.asarray(inputs["L0_attn_a_w_lo"], dtype=np.float32))
    wqkvT = np.ascontiguousarray(np.asarray(inputs["L0_attn_b_weight_qkv"], dtype=np.float32).T)
    wprojT = np.ascontiguousarray(np.asarray(inputs["L0_attn_b_weight_proj"], dtype=np.float32).T)
    wffn = np.ascontiguousarray(np.asarray(inputs["L0_ffn_weight"], dtype=np.float32))

    masks = _masks()
    oneb = np.zeros((128, 4224), dtype=np.float32)
    oneb[:, 128::132] = 1.0
    oneb = oneb.astype(ml_dtypes.bfloat16)

    nc = build()
    in_maps = []
    for c in range(NC8):
        xs = np.ascontiguousarray(
            np.concatenate([x[0, 512 * c:512 * c + 512], x[1, 512 * c:512 * c + 512]], axis=0))
        in_maps.append({
            "xin": xs, "wq": wq, "wk": wk, "wv": wv, "wlq": wlq, "wlk": wlk, "wlv": wlv,
            "wo": wo, "wlo": wlo, "wffn": wffn, "wqkvT": wqkvT, "wprojT": wprojT,
            "cosin": _rope_tables(c), "masksin": masks, "onebin": oneb,
        })
    res = run_bass_kernel_spmd(nc, in_maps, core_ids=list(range(NC8)))
    out = np.empty((2, 4096, 4096), np.float32)
    for c in range(NC8):
        o = np.asarray(res.results[c]["xout"])
        out[0, 512 * c:512 * c + 512] = o[0:512]
        out[1, 512 * c:512 * c + 512] = o[512:1024]
    return out
